# version 01
# baseline (speedup 1.0000x reference)
import math
import os
from contextlib import ExitStack

import numpy as np
import concourse.bass as bass
import concourse.bass_isa as bass_isa
import concourse.mybir as mybir
from concourse.bass_utils import run_bass_kernel_spmd

F32 = mybir.dt.float32
BF16 = mybir.dt.bfloat16
FP16 = mybir.dt.float16
FP8 = mybir.dt.float8e4
I16 = mybir.dt.int16
I32 = mybir.dt.int32
AF = mybir.ActivationFunctionType
ALU = mybir.AluOpType
AXX = mybir.AxisListType.X

DBG_STAGE = None
MOE_STOP = None
SIM_LIB = False
LN2f = math.log(2.0)
M64 = 1.5 * 2.0 ** 52
SQ2I = 2.0 ** -0.5

D, HID, T, NSEQ, NE, CAP, NH, HD, FF = 1024, 1536, 2048, 2, 16, 256, 16, 64, 4096

C_TINY38, C_SQ2I, C_ONE, C_EPS, C_B3, C_B10, C_NB3, C_NB10 = range(8)
NCONST = 8
CONST_VALS = [1e-38, SQ2I, 1.0, 1e-5, 3.0 * LN2f, 10.0 * LN2f,
              -3.0 * LN2f, -10.0 * LN2f]


class K:
    ENGS = ("t", "v", "s", "g", "y")
    SEMS = ("t", "v", "s", "g", "y", "gd", "yd")

    def __init__(self, block, sems):
        self.sems = sems
        self.cnt = {e: 0 for e in self.SEMS}
        self.eng_fn = {"t": block.tensor, "v": block.vector, "s": block.scalar,
                       "g": block.gpsimd, "y": block.sync}

    def emit(self, eng, fn, waits=(), sig=0, sem=None):
        waits = [w for w in waits if w is not None]
        sem = sem or eng
        tok = None
        if sig:
            self.cnt[sem] += sig
            tok = (sem, self.cnt[sem])
        sems = self.sems

        def body(e):
            for (weng, val) in waits:
                e.wait_ge(sems[weng], val)
            inst = fn(e)
            if sig:
                assert inst is not None
                inst.then_inc(sems[sem], sig)
        self.eng_fn[eng](body)
        return tok

    def t(self, fn, waits=(), sig=0): return self.emit("t", fn, waits, sig)
    def v(self, fn, waits=(), sig=0): return self.emit("v", fn, waits, sig)
    def s(self, fn, waits=(), sig=0): return self.emit("s", fn, waits, sig)
    def g(self, fn, waits=(), sig=0): return self.emit("g", fn, waits, sig)
    def y(self, fn, waits=(), sig=0): return self.emit("y", fn, waits, sig)

    def dma_g(self, out, in_, waits=()):
        return self.emit("g", lambda e: e.dma_start(out=out, in_=in_), waits, 16,
                         sem="gd")

    def dma_y(self, out, in_, waits=()):
        return self.emit("y", lambda e: e.dma_start(out=out, in_=in_), waits, 16,
                         sem="yd")

    def dma_yt(self, out, in_, waits=()):
        return self.emit("y", lambda e: e.dma_start(out=out, in_=in_,
                         transpose=True), waits, 16, sem="yd")

    def barrier(self):
        bar = self.bar_tiles
        dma_toks = [("gd", self.cnt["gd"]), ("yd", self.cnt["yd"])]
        toks = [("t", self.cnt["t"]), ("yd", self.cnt["yd"])]
        toks.append(self.emit("v", lambda en: en.memset(bar[0][:], 0.0), sig=1))
        toks.append(self.emit("s", lambda en: en.memzero(bar[1][:]), sig=1))
        toks.append(self.emit("g", lambda en: en.memset(bar[2][:], 0.0), sig=1))
        for e in self.ENGS:
            fn = (lambda en: None) if e in ("t", "y") else (lambda en: en.nop())
            self.emit(e, fn, waits=[t for t in toks if t[0] != e] + dma_toks)


def make_qscratch(nc, ph, name, W):
    return dict(
        A=ph.enter_context(nc.sbuf_tensor(f"q{name}_A", [128, W], BF16)),
        B=ph.enter_context(nc.sbuf_tensor(f"q{name}_B", [128, W], BF16)),
        C=ph.enter_context(nc.sbuf_tensor(f"q{name}_C", [128, W], BF16)),
        R=ph.enter_context(nc.sbuf_tensor(f"q{name}_R", [128, W], F32)),
        R2=ph.enter_context(nc.sbuf_tensor(f"q{name}_R2", [128, W], F32)),
        F8=ph.enter_context(nc.sbuf_tensor(f"q{name}_F8", [128, W], FP8)),
        F16=ph.enter_context(nc.sbuf_tensor(f"q{name}_F16", [128, W], FP16)),
    )


def emit_quant(ctx, qs, pairs, exp_bits, man_bits, signed, wait0=()):
    k, cst, kp = ctx["k"], ctx["cst"], ctx["kp"]
    cfac = (0.5 if signed else 1.0) / LN2f
    mexp = float(1 - (2 ** (exp_bits - 1) - 1))
    man = float(man_bits)
    tok = None
    src_free = []
    for (src, dst) in pairs:
        w = src.shape[-1]
        Aw, Bw, Cw, Rw, R2w = (qs[x][:, 0:w] for x in ("A", "B", "C", "R", "R2"))
        pw = list(wait0) + [tok]
        if signed:
            t0 = k.s(lambda e, src=src, Aw=Aw: e.activation(
                out=Aw, in_=src, func=AF.Square, scale=cst[:, C_SQ2I:C_SQ2I + 1]),
                waits=pw, sig=1)
            t0 = k.s(lambda e, Aw=Aw: e.activation(
                out=Aw, in_=Aw, func=AF.Ln, bias=cst[:, C_TINY38:C_TINY38 + 1]),
                waits=[t0], sig=1)
        else:
            t0 = k.s(lambda e, src=src, Aw=Aw: e.activation(
                out=Aw, in_=src, func=AF.Ln, scale=cst[:, C_SQ2I:C_SQ2I + 1],
                bias=cst[:, C_TINY38:C_TINY38 + 1]), waits=pw, sig=1)
        t1 = k.v(lambda e, Aw=Aw, Bw=Bw: e.tensor_scalar(
            out=Bw, in0=Aw, scalar1=cfac, scalar2=M64, op0=ALU.mult, op1=ALU.add),
            waits=[t0], sig=1)
        t2 = k.v(lambda e, Aw=Aw, Bw=Bw: e.tensor_scalar(
            out=Aw, in0=Bw, scalar1=-M64, scalar2=None, op0=ALU.add),
            waits=[t1], sig=1)
        t3 = k.v(lambda e, Aw=Aw, Bw=Bw: e.tensor_scalar(
            out=Bw, in0=Aw, scalar1=mexp, scalar2=None, op0=ALU.max),
            waits=[t2], sig=1)
        c_pb = C_B3 if man_bits == 3 else C_B10
        c_nb = C_NB3 if man_bits == 3 else C_NB10
        t4 = k.s(lambda e, Aw=Aw, Bw=Bw, c_pb=c_pb: e.activation(
            out=Aw, in_=Bw, func=AF.Exp, scale=-LN2f,
            bias=cst[:, c_pb:c_pb + 1]), waits=[t3], sig=1)
        t5 = k.s(lambda e, Bw=Bw, Cw=Cw, c_nb=c_nb: e.activation(
            out=Cw, in_=Bw, func=AF.Exp, scale=LN2f,
            bias=cst[:, c_nb:c_nb + 1]), waits=[t4], sig=1)
        t6 = k.v(lambda e, src=src, Aw=Aw, Rw=Rw: e.tensor_tensor(
            out=Rw, in0=src, in1=Aw, op=ALU.mult), waits=[t4], sig=1)
        t7 = k.v(lambda e, Rw=Rw, R2w=R2w: e.tensor_scalar(
            out=R2w, in0=Rw, scalar1=M64, scalar2=-M64, op0=ALU.add, op1=ALU.add),
            waits=[t6], sig=1)
        t8 = k.v(lambda e, R2w=R2w, Cw=Cw, Rw=Rw: e.tensor_tensor(
            out=Rw, in0=R2w, in1=Cw, op=ALU.mult), waits=[t7, t5], sig=1)
        if man_bits == 3:
            F8w = qs["F8"][:, 0:w]
            t9a = k.g(lambda e, src=src, F8w=F8w: e.tensor_copy(out=F8w, in_=src),
                      waits=pw, sig=1)
            t9 = k.g(lambda e, F8w=F8w, Aw=Aw: e.tensor_copy(out=Aw, in_=F8w),
                     waits=[t9a, t6], sig=1)
            acast = Aw
            src_free += [t9a]
        else:
            F16w = qs["F16"][:, 0:w]
            t9 = k.v(lambda e, src=src, F16w=F16w: e.tensor_copy(out=F16w, in_=src),
                     waits=pw, sig=1)
            acast = F16w
            src_free += [t9]
        tok = k.v(lambda e, acast=acast, Rw=Rw, dst=dst: e.scalar_tensor_tensor(
            out=dst, in0=acast, scalar=kp[:, 0:1], in1=Rw,
            op0=ALU.mult, op1=ALU.add), waits=[t8, t9], sig=1)
        src_free += [t0, t6]
    return tok, src_free


def build_nc(nblocks=4):
    nc = bass.Bass("TRN2", target_bir_lowering=False, detect_race_conditions=False)
    dd = {}

    def din(name, shape, dt=F32):
        dd[name] = nc.dram_tensor(name, shape, dt, kind="ExternalInput").ap()
        return dd[name]

    dx = din("x", [NSEQ, 8, 128, T])
    dy = nc.dram_tensor("y", [NSEQ, 8, 128, T], F32, kind="ExternalOutput").ap()
    ddbg = (nc.dram_tensor("dbg", [2, 128, 12324], F32,
                           kind="ExternalOutput").ap()
            if DBG_STAGE is not None else None)
    din("iwT", [24, 8, 128, 128])
    din("gwT", [24, 12, 128, 128])
    din("owT", [8, 12, 128, 128])
    din("cdiag", [12, 4, 128, 128])
    din("cb", [128, 12])
    din("fb", [128, 12])
    din("wrt", [8, 128, 16])
    din("wg", [NE, 4, 8, 128, 128])
    din("wf", [NE, 4, 8, 128, 128])
    din("wp", [NE, 4, 128, 1024])
    din("qkT", [NH, 8, 128, 128])
    din("vT", [8, 128, 1024])
    din("woT", [8, 16, 64, 128])
    din("w1T", [32, 8, 128, 128])
    din("w2T", [32, 128, 1024])
    din("b1", [128, 32])
    din("b2", [128, 8])
    din("consts", [128, NCONST])
    din("ones_col", [128, 1])
    din("onesq", [128, 128])
    din("ident", [128, 128])
    din("masks", [128, 4 * 512])
    din("iota16", [128, 16])

    d_bufq = nc.dram_tensor("bufq", [NE * CAP + 128, D], BF16, kind="Internal").ap()
    d_oall = nc.dram_tensor("oall", [NE * CAP, D], BF16, kind="Internal").ap()
    d_wqd = nc.dram_tensor("wqd", [8, 32, 128, 128], BF16, kind="Internal").ap()

    xs = [nc.alloc_sbuf_tensor(f"x{c}", [128, T], F32) for c in range(8)]
    cst = nc.alloc_sbuf_tensor("cst", [128, NCONST], F32)
    kp = nc.alloc_sbuf_tensor("kp", [128, 2], F32)
    onec = nc.alloc_sbuf_tensor("onec", [128, 1], BF16)
    onesq = nc.alloc_sbuf_tensor("onesq_b", [128, 128], BF16)
    onesf = nc.alloc_sbuf_tensor("onesf_f", [128, 128], F32)
    idf = nc.alloc_sbuf_tensor("idf", [128, 128], F32)
    cvec = nc.alloc_sbuf_tensor("cvec", [128, 12], F32)
    cbt = nc.alloc_sbuf_tensor("cbt", [128, 12], F32)
    b1t = nc.alloc_sbuf_tensor("b1t", [128, 32], F32)
    b2q = nc.alloc_sbuf_tensor("b2q", [128, 8], F32)
    iota = nc.alloc_sbuf_tensor("iota", [128, 16], F32)
    masks = nc.alloc_sbuf_tensor("masksb", [128, 4 * 512], BF16)
    rowsc = nc.alloc_sbuf_tensor("rowsc", [128, 1024], F32)
    rowbf = nc.alloc_sbuf_tensor("rowbf", [128, 1024], BF16)
    tail = nc.alloc_sbuf_tensor("hktail", [128, 36], BF16)

    es = ExitStack()
    sems = {e: es.enter_context(nc.semaphore(f"sem_{e}")) for e in K.SEMS}

    bar_tiles = [nc.alloc_sbuf_tensor(f"bar{i}", [128, 1], F32) for i in range(3)]
    with nc.Block() as block:
        k = K(block, sems)
        k.bar_tiles = bar_tiles
        ctx = dict(nc=nc, k=k, d=dd, xs=xs, cst=cst, kp=kp, onec=onec,
                   onesq=onesq, onesf=onesf, idf=idf, cvec=cvec, cbt=cbt,
                   b1t=b1t, b2q=b2q, iota=iota, masks=masks,
                   rowsc=rowsc, rowbf=rowbf, tail=tail,
                   bufq=d_bufq, oall=d_oall, wqd=d_wqd, dbg=ddbg)

        if SIM_LIB or os.environ.get("BASS_SIM_FP64") == "1":
            k.g(lambda e: e.add_instruction(
                bass_isa.InstPseudoReloadLibraryIndex(
                    name=nc.get_next_instruction_name(), lib_index=3,
                    ins=[], outs=[])))
        tc0 = k.dma_y(cst[:], dd["consts"])
        k.dma_y(idf[:], dd["ident"])
        k.dma_y(onesf[:], dd["onesq"])
        k.dma_g(onec[:], dd["ones_col"])
        k.dma_g(onesq[:], dd["onesq"])
        tfb = k.dma_y(cvec[:], dd["fb"])
        k.dma_y(cbt[:], dd["cb"])
        k.dma_y(b1t[:], dd["b1"])
        k.dma_y(iota[:], dd["iota16"])
        k.dma_g(masks[:], dd["masks"])

        t1 = k.v(lambda e: e.tensor_scalar(out=kp[:, 1:2], in0=cst[:, C_ONE:C_ONE + 1],
                 scalar1=1e-10, scalar2=1.0, op0=ALU.add, op1=ALU.subtract),
                 waits=[tc0], sig=1)
        t2 = k.v(lambda e: e.tensor_scalar(out=kp[:, 1:2], in0=kp[:, 1:2],
                 scalar1=1e12, scalar2=1.0, op0=ALU.mult, op1=ALU.min),
                 waits=[t1], sig=1)
        k.v(lambda e: e.tensor_scalar(out=kp[:, 0:1], in0=kp[:, 1:2],
            scalar1=1.0, scalar2=-1.0, op0=ALU.subtract, op1=ALU.mult),
            waits=[t2], sig=1)
        tsp = k.s(lambda e: e.activation(out=cvec[:], in_=cvec[:], func=AF.Exp),
                  waits=[tfb], sig=1)
        tsp = k.s(lambda e: e.activation(out=cvec[:], in_=cvec[:], func=AF.Ln,
                  bias=cst[:, C_ONE:C_ONE + 1]), waits=[tsp], sig=1)
        k.v(lambda e: e.tensor_scalar(out=cvec[:], in0=cvec[:], scalar1=-8.0,
            scalar2=None, op0=ALU.mult), waits=[tsp], sig=1)

        if nblocks >= 2:
            ph_z = ExitStack()
            zb = ph_z.enter_context(nc.sbuf_tensor("zbuf", [128, D], BF16))
            tz = k.v(lambda e: e.memset(zb[:], 0.0), sig=1)
            for i in range((NE * CAP + 128) // 128):
                k.dma_y(d_bufq[i * 128:(i + 1) * 128], zb[:], waits=[tz])
            k.barrier()
            ph_z.close()

        k.barrier()
        if nblocks >= 4:
            build_w2q(ctx)
            k.barrier()

        for seq in range(NSEQ):
            for c in range(8):
                k.dma_y(xs[c][:], dx[seq, c])
            k.barrier()
            nb = [1, 2, 3, 4][:nblocks]
            for b in nb:
                if b == 1:
                    build_hawk(ctx, seq)
                elif b == 2:
                    build_moe(ctx, seq)
                elif b == 3:
                    build_attn(ctx, seq)
                elif b == 4:
                    build_ffn(ctx, seq)
                k.barrier()
                build_ln(ctx, seq, b)
                k.barrier()
            for c in range(8):
                k.dma_y(dy[seq, c], xs[c][:])
            k.barrier()
    es.close()
    return nc


def build_hawk(ctx, seq):
    nc, k = ctx["nc"], ctx["k"]
    xs, cst, cvec, cbt, d, tail = (ctx["xs"], ctx["cst"], ctx["cvec"], ctx["cbt"],
                                   ctx["d"], ctx["tail"])
    TH = 1024
    XHW = TH + 3
    ph = ExitStack()
    sb = lambda n, sh, dt: ph.enter_context(nc.sbuf_tensor(f"hk{seq}_{n}", sh, dt))
    g_buf = sb("g", [128, 12 * TH], BF16)
    xh = sb("xh", [128, 12 * XHW], BF16)
    la = sb("la", [128, 12 * TH], BF16)
    si = sb("si", [128, 12 * TH], BF16)
    xbf = [sb(f"xb{c}", [128, TH], BF16) for c in range(8)]
    slab = [sb(f"sl{i}", [128, 12 * 128], BF16) for i in range(2)]
    alpha = sb("al", [128, TH], F32)
    y1 = sb("y1", [128, TH], F32)
    hsc = sb("h", [128, TH], F32)
    carry = sb("cy", [128, 12], F32)
    ge = sb("ge", [128, 512], BF16)
    ps = [ph.enter_context(nc.psum_tensor(f"hk{seq}_p{i}", [128, 512], F32))
          for i in range(4)]

    slab_free = [None, None]
    bank_free = [None] * 4
    tcy = k.v(lambda e: e.memset(carry[:], 0.0), sig=1)
    last_out = None
    stash_tok = None

    for half in range(2):
        t0 = half * TH
        xb_tok = None
        for c in range(8):
            xb_tok = k.v(lambda e, c=c: e.tensor_copy(
                out=xbf[c][:], in_=xs[c][:, t0:t0 + TH]), waits=[last_out], sig=1)
        if half == 0:
            ttl = k.v(lambda e: e.memset(
                xh[:].rearrange("p (cc n) -> p cc n", n=XHW)[:, :, 0:3], 0.0), sig=1)
        else:
            ttl = k.v(lambda e: e.tensor_copy(
                out=xh[:].rearrange("p (cc n) -> p cc n", n=XHW)[:, :, 0:3],
                in_=tail[:].rearrange("p (cc n) -> p cc n", n=3)),
                waits=[stash_tok], sig=1)


        prev_tg = None
        for oc in range(24):
            sl = slab[oc % 2]
            tsl = k.dma_g(sl[:, 0:8 * 128].rearrange("p (kc n) -> p kc n", n=128),
                          d["iwT"][oc].rearrange("kc p n -> p kc n"),
                          waits=[slab_free[oc % 2]])
            tmm = None
            for tc in range(4):
                bk = (oc * 4 + tc) % 2
                w0 = [tsl, xb_tok, bank_free[bk], ttl]
                for kc in range(8):
                    tmm = k.t(lambda e, sl=sl, kc=kc, tc=tc, bk=bk: e.matmul(
                        ps[bk][:, 0:256], sl[:, kc * 128:(kc + 1) * 128],
                        xbf[kc][:, tc * 256:(tc + 1) * 256],
                        start=(kc == 0), stop=(kc == 7), skip_group_check=True),
                        waits=w0 if kc == 0 else (), sig=1 if kc == 7 else 0)
                if oc < 12:
                    tg = k.s(lambda e, oc=oc, tc=tc, bk=bk: e.activation(
                        out=g_buf[:, oc * TH + tc * 256: oc * TH + (tc + 1) * 256],
                        in_=ps[bk][:, 0:256], func=AF.Gelu), waits=[tmm], sig=1)
                    bank_free[bk] = tg
                    prev_tg = tg
                else:
                    cc = oc - 12
                    tx = k.s(lambda e, cc=cc, tc=tc, bk=bk: e.activation(
                        out=xh[:, cc * XHW + 3 + tc * 256: cc * XHW + 3 + (tc + 1) * 256],
                        in_=ps[bk][:, 0:256], func=AF.Copy), waits=[tmm], sig=1)
                    bank_free[bk] = tx
            slab_free[oc % 2] = tmm

        stash_tok = k.v(lambda e: e.tensor_copy(
            out=tail[:].rearrange("p (cc n) -> p cc n", n=3),
            in_=xh[:].rearrange("p (cc n) -> p cc n", n=XHW)[:, :, TH:TH + 3]),
            waits=[w for w in bank_free], sig=1)
        if DBG_STAGE in ("xh_raw", "gelu") and seq == 0:
            k.barrier()
            buf = xh if DBG_STAGE == "xh_raw" else g_buf
            wdt = 12 * XHW if DBG_STAGE == "xh_raw" else 12 * TH
            k.dma_g(ctx["dbg"][half, :, 0:wdt], buf[:, 0:wdt])
            k.barrier()

        conv_done = []
        for cc in range(12):
            cd = slab[cc % 2]
            tcd = k.dma_g(cd[:, 0:512].rearrange("p (kk n) -> p kk n", n=128),
                          d["cdiag"][cc].rearrange("kk p n -> p kk n"),
                          waits=[slab_free[cc % 2]])
            tmms = []
            for hn in range(2):
                bk = 2 + hn
                base = cc * XHW + hn * 512
                w0 = [tcd, bank_free[bk], stash_tok]
                for kk in range(4):
                    tcv = k.t(lambda e, cd=cd, kk=kk, base=base, bk=bk: e.matmul(
                        ps[bk][:, 0:512], cd[:, kk * 128:(kk + 1) * 128],
                        xh[:, base + kk: base + kk + 512],
                        start=(kk == 0), stop=(kk == 3), skip_group_check=True),
                        waits=w0 if kk == 0 else (), sig=1 if kk == 3 else 0)
                tmms.append(tcv)
            for hn in range(2):
                bk = 2 + hn
                base = cc * XHW + hn * 512
                tev = k.s(lambda e, base=base, bk=bk, cc=cc: e.activation(
                    out=xh[:, base + 3: base + 3 + 512], in_=ps[bk][:, 0:512],
                    func=AF.Identity, bias=cbt[:, cc:cc + 1]), waits=[tmms[1]], sig=1)
                bank_free[bk] = tev
                conv_done.append(tev)
            slab_free[cc % 2] = tmms[1]

        if DBG_STAGE == "xhc" and seq == 0:
            k.barrier()
            k.dma_g(ctx["dbg"][half, :, 0:12 * XHW], xh[:])
            k.barrier()
        for fi in range(2):
            for oc in range(12):
                si_ = (fi * 12 + oc) % 2
                sl = slab[si_]
                tsl = k.dma_g(sl[:].rearrange("p (kc n) -> p kc n", n=128),
                              d["gwT"][fi * 12 + oc].rearrange("kc p n -> p kc n"),
                              waits=[slab_free[si_]])
                tmm = None
                for tc in range(4):
                    bk = (oc * 4 + tc) % 2
                    w0 = [tsl, bank_free[bk]] + \
                        (conv_done if oc == 0 and tc == 0 and fi == 0 else [])
                    for kc in range(12):
                        tmm = k.t(lambda e, sl=sl, kc=kc, tc=tc, bk=bk: e.matmul(
                            ps[bk][:, 0:256], sl[:, kc * 128:(kc + 1) * 128],
                            xh[:, kc * XHW + 3 + tc * 256: kc * XHW + 3 + (tc + 1) * 256],
                            start=(kc == 0), stop=(kc == 11), skip_group_check=True),
                            waits=w0 if kc == 0 else (), sig=1 if kc == 11 else 0)
                    dst = la if fi == 0 else si
                    tsg = k.s(lambda e, dst=dst, oc=oc, tc=tc, bk=bk: e.activation(
                        out=dst[:, oc * TH + tc * 256: oc * TH + (tc + 1) * 256],
                        in_=ps[bk][:, 0:256], func=AF.Sigmoid), waits=[tmm], sig=1)
                    bank_free[bk] = tsg
                slab_free[si_] = tmm

        if DBG_STAGE in ("la", "si") and seq == 0:
            k.barrier()
            k.dma_g(ctx["dbg"][half, :, 0:12 * TH], (la if DBG_STAGE == "la" else si)[:])
            k.barrier()
        gh_toks = []
        for cc in range(12):
            sla = la[:, cc * TH:(cc + 1) * TH]
            ssi = si[:, cc * TH:(cc + 1) * TH]
            sxh = xh[:, cc * XHW + 3: cc * XHW + 3 + TH]
            sg = g_buf[:, cc * TH:(cc + 1) * TH]
            tla = k.v(lambda e, sla=sla, cc=cc: e.tensor_scalar(
                out=hsc[:], in0=sla, scalar1=cvec[:, cc:cc + 1], scalar2=None,
                op0=ALU.mult),
                waits=[bank_free[0], bank_free[1]] if cc == 0 else (), sig=1)
            tal = k.s(lambda e: e.activation(out=alpha[:], in_=hsc[:],
                      func=AF.Exp), waits=[tla], sig=1)
            ta2 = k.s(lambda e: e.activation(out=y1[:], in_=hsc[:],
                      func=AF.Exp, scale=2.0), waits=[tla], sig=1)
            ty1 = k.v(lambda e: e.tensor_scalar(out=y1[:], in0=y1[:], scalar1=-1.0,
                      scalar2=1.0, op0=ALU.mult, op1=ALU.add), waits=[ta2], sig=1)
            tln = k.s(lambda e: e.activation(out=y1[:], in_=y1[:], func=AF.Ln,
                      bias=cst[:, C_TINY38:C_TINY38 + 1]), waits=[ty1], sig=1)
            tbt = k.s(lambda e: e.activation(out=y1[:], in_=y1[:], func=AF.Exp,
                      scale=0.5), waits=[tln], sig=1)
            tx1 = k.v(lambda e, ssi=ssi: e.tensor_tensor(out=y1[:], in0=y1[:],
                      in1=ssi, op=ALU.mult), waits=[tbt], sig=1)
            txi = k.v(lambda e, ssi=ssi, sxh=sxh: e.tensor_tensor(out=ssi, in0=y1[:],
                      in1=sxh, op=ALU.mult), waits=[tx1], sig=1)
            init = 0.0 if half == 0 else carry[:, cc:cc + 1]
            th = k.v(lambda e, ssi=ssi, init=init: e.tensor_tensor_scan(
                out=hsc[:], data0=alpha[:], data1=ssi, initial=init,
                op0=ALU.mult, op1=ALU.add), waits=[tal, txi, tcy], sig=1)
            tcy = k.v(lambda e, cc=cc: e.tensor_copy(out=carry[:, cc:cc + 1],
                      in_=hsc[:, TH - 1:TH]), waits=[th], sig=1)
            tgh = k.v(lambda e, sg=sg, sxh=sxh: e.tensor_tensor(out=sxh, in0=sg,
                      in1=hsc[:], op=ALU.mult), waits=[tcy], sig=1)
            gh_toks.append(tgh)

        if DBG_STAGE == "gh" and seq == 0:
            k.barrier()
            k.dma_g(ctx["dbg"][half, :, 0:12 * XHW], xh[:])
            k.barrier()
        if DBG_STAGE == "xi" and seq == 0:
            k.barrier()
            k.dma_g(ctx["dbg"][half, :, 0:12 * TH], si[:])
            k.dma_g(ctx["dbg"][half, :, 12 * TH:12 * TH + 12], cvec[:])
            k.barrier()
        for oc in range(8):
            sl = slab[oc % 2]
            tsl = k.dma_g(sl[:].rearrange("p (kc n) -> p kc n", n=128),
                          d["owT"][oc].rearrange("kc p n -> p kc n"),
                          waits=[slab_free[oc % 2]])
            tmm = None
            for t2i in range(2):
                bk = (oc * 2 + t2i) % 4
                w0 = [tsl, bank_free[bk]] + (gh_toks if oc == 0 and t2i == 0 else [])
                for kc in range(12):
                    tmm = k.t(lambda e, sl=sl, kc=kc, t2i=t2i, bk=bk: e.matmul(
                        ps[bk][:, 0:512], sl[:, kc * 128:(kc + 1) * 128],
                        xh[:, kc * XHW + 3 + t2i * 512: kc * XHW + 3 + (t2i + 1) * 512],
                        start=(kc == 0), stop=(kc == 11), skip_group_check=True),
                        waits=w0 if kc == 0 else (), sig=1 if kc == 11 else 0)
                tadd = k.v(lambda e, oc=oc, t2i=t2i, bk=bk: e.scalar_tensor_tensor(
                    out=xs[oc][:, t0 + t2i * 512: t0 + (t2i + 1) * 512],
                    in0=ps[bk][:, 0:512], scalar=1.0,
                    in1=xs[oc][:, t0 + t2i * 512: t0 + (t2i + 1) * 512],
                    op0=ALU.mult, op1=ALU.add), waits=[tmm], sig=1)
                bank_free[bk] = tadd
                last_out = tadd
            slab_free[oc % 2] = tmm
    ph.close()


def build_ln(ctx, seq, which):
    nc, k = ctx["nc"], ctx["k"]
    xs, cst, onec, onesq = ctx["xs"], ctx["cst"], ctx["onec"], ctx["onesq"]
    rowsc, rowbf = ctx["rowsc"], ctx["rowbf"]
    ph = ExitStack()
    xb = ph.enter_context(nc.sbuf_tensor(f"ln{seq}{which}_xb", [128, 8 * 512], BF16))
    sq = ph.enter_context(nc.sbuf_tensor(f"ln{seq}{which}_sq", [128, 8 * 512], BF16))
    u = ph.enter_context(nc.sbuf_tensor(f"ln{seq}{which}_u", [128, 512], F32))
    pA = ph.enter_context(nc.psum_tensor(f"ln{seq}{which}_A", [1, 512], F32))
    pB = ph.enter_context(nc.psum_tensor(f"ln{seq}{which}_B", [1, 512], F32))
    pBA = ph.enter_context(nc.psum_tensor(f"ln{seq}{which}_BA", [128, 512], F32))
    pBB = ph.enter_context(nc.psum_tensor(f"ln{seq}{which}_BB", [128, 512], F32))

    prev_apply = None
    prev_rows = None
    for sl in range(4):
        c0 = sl * 512
        cast_toks = []
        for kc in range(8):
            txb = k.v(lambda e, kc=kc, c0=c0: e.tensor_copy(
                out=xb[:, kc * 512:(kc + 1) * 512], in_=xs[kc][:, c0:c0 + 512]),
                waits=[prev_apply] if kc == 0 else (), sig=1)
            tsq = k.s(lambda e, kc=kc, c0=c0: e.activation(
                out=sq[:, kc * 512:(kc + 1) * 512], in_=xs[kc][:, c0:c0 + 512],
                func=AF.Square), waits=[prev_apply] if kc == 0 else (), sig=1)
            k.t(lambda e, kc=kc: e.matmul(pA[:], onec[:], xb[:, kc * 512:(kc + 1) * 512],
                start=(kc == 0), stop=(kc == 7), skip_group_check=True),
                waits=[txb] + ([prev_rows] if kc == 0 else []))
            tmm = k.t(lambda e, kc=kc: e.matmul(pB[:], onec[:], sq[:, kc * 512:(kc + 1) * 512],
                start=(kc == 0), stop=(kc == 7), skip_group_check=True), waits=[tsq],
                sig=1 if kc == 7 else 0)
        r_m = rowsc[0:1, 0:512]
        r_v = rowsc[0:1, 512:1024]
        t1 = k.v(lambda e: e.tensor_scalar(out=r_m, in0=pA[:], scalar1=1.0 / D,
                 scalar2=None, op0=ALU.mult), waits=[tmm], sig=1)
        t2 = k.v(lambda e: e.tensor_tensor(out=r_v, in0=r_m, in1=r_m, op=ALU.mult),
                 waits=[t1], sig=1)
        t3 = k.v(lambda e: e.scalar_tensor_tensor(out=r_v, in0=pB[:],
                 scalar=1.0 / D, in1=r_v, op0=ALU.mult, op1=ALU.subtract),
                 waits=[t2], sig=1)
        t4 = k.s(lambda e: e.activation(out=r_v, in_=r_v, func=AF.Ln,
                 bias=cst[0:1, C_EPS:C_EPS + 1]), waits=[t3], sig=1)
        t5 = k.s(lambda e: e.activation(out=r_v, in_=r_v, func=AF.Exp,
                 scale=-0.5), waits=[t4], sig=1)
        t6 = k.v(lambda e: e.tensor_copy(out=rowbf[0:1, 0:512], in_=r_v),
                 waits=[t5], sig=1)
        t7 = k.v(lambda e: e.scalar_tensor_tensor(out=rowbf[0:1, 512:1024],
                 in0=r_m, scalar=-1.0, in1=r_v, op0=ALU.mult, op1=ALU.mult),
                 waits=[t6], sig=1)
        prev_rows = t7
        tba = k.t(lambda e: e.matmul(pBA[:], onesq[0:1, :], rowbf[0:1, 0:512],
                  start=True, stop=True, skip_group_check=True), waits=[t7], sig=1)
        tbb = k.t(lambda e: e.matmul(pBB[:], onesq[0:1, :], rowbf[0:1, 512:1024],
                  start=True, stop=True, skip_group_check=True), waits=[t7], sig=1)
        for kc in range(8):
            ta = k.v(lambda e, kc=kc, c0=c0: e.tensor_tensor(
                out=u[:], in0=xs[kc][:, c0:c0 + 512], in1=pBA[:], op=ALU.mult),
                waits=[tba, tbb], sig=1)
            prev_apply = k.v(lambda e, kc=kc, c0=c0: e.scalar_tensor_tensor(
                out=xs[kc][:, c0:c0 + 512], in0=u[:], scalar=1.0, in1=pBB[:],
                op0=ALU.mult, op1=ALU.add), waits=[ta], sig=1)
    ph.close()


def build_moe(ctx, seq):
    nc, k = ctx["nc"], ctx["k"]
    xs, cst, idf, iota = ctx["xs"], ctx["cst"], ctx["idf"], ctx["iota"]
    onesf = ctx["onesf"]
    d, bufq, oall = ctx["d"], ctx["bufq"], ctx["oall"]
    ph = ExitStack()
    sb = lambda n, sh, dt: ph.enter_context(nc.sbuf_tensor(f"mo{seq}_{n}", sh, dt))

    prb = sb("prb", [128, 2 * 16], F32)
    idxi = sb("idxi", [128, 4 * 16], I32)
    pT = ph.enter_context(nc.psum_tensor(f"mo{seq}_pT", [128, 512], F32))
    pC = ph.enter_context(nc.psum_tensor(f"mo{seq}_pC", [128, 16], F32))

    ph_r = ExitStack()
    sbr = lambda n, sh, dt: ph_r.enter_context(nc.sbuf_tensor(f"mor{seq}_{n}", sh, dt))
    x_tm = sbr("xtm", [128, 16 * D], BF16)
    g_sb = sbr("g", [16, T], F32)
    g_tm = sbr("gtm", [128, 16 * 16], F32)
    msk = sbr("msk", [128, 2 * 256], F32)
    mfm = sbr("mfm", [16, 2 * T], F32)
    csf = sbr("csf", [16, 2 * T], F32)
    cstm = sbr("cstm", [128, 2 * 256], F32)
    c0bc = sbr("c0bc", [128, 16], F32)
    sca = sbr("sca", [128, 10 * 16], F32)
    idxf = sbr("idxf", [128, 4 * 16], F32)
    wrt = sbr("wrt", [128, 8 * 16], BF16)
    xbs = sbr("xbs", [128, 8 * 512], BF16)
    pR = ph_r.enter_context(nc.psum_tensor(f"mo{seq}_pR", [16, 512], F32))

    co = lambda j, tc: sca[:, j * 16 + tc: j * 16 + tc + 1]

    def stop_here():
        k.barrier()
        ph_r.close()
        ph.close()

    if MOE_STOP == "zt":
        stop_here()
        return
    tprev = None
    for tc in range(16):
        for kc in range(8):
            tt = k.t(lambda e, tc=tc, kc=kc: e.transpose(
                pT[:, 0:128], xs[kc][:, tc * 128:(tc + 1) * 128], idf[:]),
                waits=[tprev], sig=1)
            tprev = k.v(lambda e, tc=tc, kc=kc: e.tensor_copy(
                out=x_tm[:, tc * D + kc * 128: tc * D + (kc + 1) * 128],
                in_=pT[:, 0:128]), waits=[tt], sig=1)
    t_xtm = tprev
    if MOE_STOP == "xtm":
        stop_here()
        return

    twr = k.dma_g(wrt[:].rearrange("p (kc n) -> p kc n", n=16), d["wrt"].rearrange("kc p n -> p kc n"))
    tgev = None
    for sl in range(4):
        for kc in range(8):
            txb = k.v(lambda e, kc=kc, sl=sl: e.tensor_copy(
                out=xbs[:, kc * 512:(kc + 1) * 512],
                in_=xs[kc][:, sl * 512:(sl + 1) * 512]),
                waits=[tgev] if kc == 0 else (), sig=1)
            tmm = k.t(lambda e, kc=kc: e.matmul(pR[:], wrt[:, kc * 16:(kc + 1) * 16],
                xbs[:, kc * 512:(kc + 1) * 512], start=(kc == 0), stop=(kc == 7),
                skip_group_check=True), waits=[txb, twr], sig=1 if kc == 7 else 0)
        tgev = k.v(lambda e, sl=sl: e.tensor_copy(
            out=g_sb[:, sl * 512:(sl + 1) * 512], in_=pR[:]), waits=[tmm], sig=1)

    if MOE_STOP == "router":
        stop_here()
        return
    tprev = tgev
    for tc in range(16):
        tt = k.t(lambda e, tc=tc: e.transpose(pC[:, 0:16],
                 g_sb[:, tc * 128:(tc + 1) * 128], idf[0:16, 0:16]),
                 waits=[tprev], sig=1)
        tprev = k.v(lambda e, tc=tc: e.tensor_copy(
            out=g_tm[:, tc * 16:(tc + 1) * 16], in_=pC[:, 0:16]), waits=[tt], sig=1)

    if MOE_STOP == "gtm":
        stop_here()
        return
    gex = sbr("gex", [128, 16 * 16], F32)
    prev = k.s(lambda e: e.activation(out=gex[:], in_=g_tm[:], func=AF.Exp),
               waits=[tprev], sig=1)
    for tc in range(16):
        gt = gex[:, tc * 16:(tc + 1) * 16]
        m1t = msk[:, tc * 16:(tc + 1) * 16]
        m2t = msk[:, 256 + tc * 16: 256 + (tc + 1) * 16]
        prev = k.v(lambda e, gt=gt, tc=tc: e.tensor_reduce(
            out=co(0, tc), in_=gt, axis=AXX, op=ALU.add), waits=[prev], sig=1)
        prev = k.v(lambda e, gt=gt, tc=tc: e.reduce_max(
            out=co(1, tc), in_=gt, axis=AXX), waits=[prev], sig=1)
        prev = k.v(lambda e, gt=gt, m1t=m1t, tc=tc: e.tensor_scalar(
            out=m1t, in0=gt, scalar1=co(1, tc), scalar2=None, op0=ALU.is_ge),
            waits=[prev], sig=1)
        prev = k.v(lambda e, m1t=m1t, m2t=m2t: e.tensor_scalar(
            out=m2t, in0=m1t, scalar1=-1.0, scalar2=1.0, op0=ALU.mult, op1=ALU.add),
            waits=[prev], sig=1)
        prev = k.v(lambda e, gt=gt, m2t=m2t: e.tensor_tensor(
            out=m2t, in0=gt, in1=m2t, op=ALU.mult), waits=[prev], sig=1)
        prev = k.v(lambda e, m2t=m2t, tc=tc: e.reduce_max(
            out=co(2, tc), in_=m2t, axis=AXX), waits=[prev], sig=1)
        prev = k.v(lambda e, m2t=m2t, tc=tc: e.tensor_scalar(
            out=m2t, in0=m2t, scalar1=co(2, tc), scalar2=None, op0=ALU.is_ge),
            waits=[prev], sig=1)
        prev = k.v(lambda e, gt=gt, m1t=m1t: e.tensor_tensor(
            out=gt, in0=m1t, in1=iota[:], op=ALU.mult), waits=[prev], sig=1)
        prev = k.v(lambda e, gt=gt, tc=tc: e.tensor_reduce(
            out=co(3, tc), in_=gt, axis=AXX, op=ALU.add), waits=[prev], sig=1)
        prev = k.v(lambda e, gt=gt, m2t=m2t: e.tensor_tensor(
            out=gt, in0=m2t, in1=iota[:], op=ALU.mult), waits=[prev], sig=1)
        prev = k.v(lambda e, gt=gt, tc=tc: e.tensor_reduce(
            out=co(4, tc), in_=gt, axis=AXX, op=ALU.add), waits=[prev], sig=1)
        prev = k.v(lambda e, tc=tc: e.reciprocal(out=co(5, tc), in_=co(0, tc)),
                   waits=[prev], sig=1)
        prev = k.v(lambda e, tc=tc: e.tensor_tensor(out=prb[:, tc:tc + 1],
                   in0=co(1, tc), in1=co(5, tc), op=ALU.mult), waits=[prev], sig=1)
        prev = k.v(lambda e, tc=tc: e.tensor_tensor(out=prb[:, 16 + tc:17 + tc],
                   in0=co(2, tc), in1=co(5, tc), op=ALU.mult), waits=[prev], sig=1)

    if MOE_STOP == "math":
        stop_here()
        return
    for ki in range(2):
        for tc in range(16):
            tt = k.t(lambda e, ki=ki, tc=tc: e.transpose(
                pT[0:16, 0:128], msk[:, ki * 256 + tc * 16: ki * 256 + (tc + 1) * 16],
                idf[:]), waits=[prev], sig=1)
            prev = k.v(lambda e, ki=ki, tc=tc: e.tensor_copy(
                out=mfm[:, ki * T + tc * 128: ki * T + (tc + 1) * 128],
                in_=pT[0:16, 0:128]), waits=[tt], sig=1)
    tzr = k.v(lambda e: e.memset(csf[:, 0:T], 0.0), sig=1)
    for ki in range(2):
        prev = k.v(lambda e, ki=ki: e.tensor_tensor_scan(
            out=csf[:, ki * T:(ki + 1) * T], data0=mfm[:, ki * T:(ki + 1) * T],
            data1=mfm[:, ki * T:(ki + 1) * T], initial=0.0,
            op0=ALU.add, op1=ALU.max), waits=[prev, tzr], sig=1)
    for ki in range(2):
        for tc in range(16):
            tt = k.t(lambda e, ki=ki, tc=tc: e.transpose(
                pC[:, 0:16], csf[:, ki * T + tc * 128: ki * T + (tc + 1) * 128],
                idf[0:16, 0:16]), waits=[prev], sig=1)
            prev = k.v(lambda e, ki=ki, tc=tc: e.tensor_copy(
                out=cstm[:, ki * 256 + tc * 16: ki * 256 + (tc + 1) * 16],
                in_=pC[:, 0:16]), waits=[tt], sig=1)
    tdm = k.dma_y(ctx["rowsc"][0:1, 0:16], cstm[127:128, 15 * 16:256],
                  waits=[prev])
    tc0b = k.t(lambda e: e.matmul(pC[:, 0:16], onesf[0:1, :],
               ctx["rowsc"][0:1, 0:16], start=True, stop=True,
               skip_group_check=True), waits=[tdm], sig=1)
    prev = k.v(lambda e: e.tensor_copy(out=c0bc[:], in_=pC[:, 0:16]),
               waits=[tc0b], sig=1)

    for tc in range(16):
        m1t = msk[:, tc * 16:(tc + 1) * 16]
        m2t = msk[:, 256 + tc * 16: 256 + (tc + 1) * 16]
        c1t = cstm[:, tc * 16:(tc + 1) * 16]
        c2t = cstm[:, 256 + tc * 16: 256 + (tc + 1) * 16]
        gt = gex[:, tc * 16:(tc + 1) * 16]
        prev = k.v(lambda e, m1t=m1t, c1t=c1t, gt=gt: e.tensor_tensor(
            out=gt, in0=m1t, in1=c1t, op=ALU.mult), waits=[prev], sig=1)
        prev = k.v(lambda e, gt=gt, tc=tc: e.tensor_reduce(
            out=co(6, tc), in_=gt, axis=AXX, op=ALU.add), waits=[prev], sig=1)
        prev = k.v(lambda e, tc=tc: e.tensor_scalar(out=co(6, tc), in0=co(6, tc),
                   scalar1=1.0, scalar2=None, op0=ALU.subtract), waits=[prev], sig=1)
        prev = k.v(lambda e, m2t=m2t, c2t=c2t, gt=gt: e.tensor_tensor(
            out=gt, in0=m2t, in1=c2t, op=ALU.mult), waits=[prev], sig=1)
        prev = k.v(lambda e, gt=gt, tc=tc: e.tensor_reduce(
            out=co(7, tc), in_=gt, axis=AXX, op=ALU.add), waits=[prev], sig=1)
        prev = k.v(lambda e, m2t=m2t, gt=gt: e.tensor_tensor(
            out=gt, in0=m2t, in1=c0bc[:], op=ALU.mult), waits=[prev], sig=1)
        prev = k.v(lambda e, gt=gt, tc=tc: e.tensor_reduce(
            out=co(8, tc), in_=gt, axis=AXX, op=ALU.add), waits=[prev], sig=1)
        prev = k.v(lambda e, tc=tc: e.tensor_scalar(out=co(7, tc), in0=co(7, tc),
                   scalar1=1.0, scalar2=None, op0=ALU.subtract), waits=[prev], sig=1)
        prev = k.v(lambda e, tc=tc: e.tensor_tensor(out=co(7, tc), in0=co(7, tc),
                   in1=co(8, tc), op=ALU.add), waits=[prev], sig=1)
        for ki in range(2):
            pos = co(6 + ki, tc)
            ei = co(3 + ki, tc)
            prev = k.v(lambda e, pos=pos, tc=tc: e.tensor_scalar(
                out=co(9, tc), in0=pos, scalar1=255.0, scalar2=None, op0=ALU.min),
                waits=[prev], sig=1)
            prev = k.v(lambda e, ei=ei, tc=tc, ki=ki: e.scalar_tensor_tensor(
                out=idxf[:, (2 + ki) * 16 + tc: (2 + ki) * 16 + tc + 1], in0=ei,
                scalar=256.0, in1=co(9, tc), op0=ALU.mult, op1=ALU.add),
                waits=[prev], sig=1)
            prev = k.v(lambda e, pos=pos, tc=tc: e.tensor_scalar(
                out=co(9, tc), in0=pos, scalar1=255.5, scalar2=None, op0=ALU.is_le),
                waits=[prev], sig=1)
            sidx = idxf[:, ki * 16 + tc: ki * 16 + tc + 1]
            prev = k.v(lambda e, ei=ei, pos=pos, sidx=sidx: e.scalar_tensor_tensor(
                out=sidx, in0=ei, scalar=256.0, in1=pos, op0=ALU.mult, op1=ALU.add),
                waits=[prev], sig=1)
            prev = k.v(lambda e, sidx=sidx: e.tensor_scalar(
                out=sidx, in0=sidx, scalar1=4096.0, scalar2=None, op0=ALU.subtract),
                waits=[prev], sig=1)
            prev = k.v(lambda e, sidx=sidx, tc=tc: e.scalar_tensor_tensor(
                out=sidx, in0=sidx, scalar=1.0, in1=co(9, tc), op0=ALU.bypass,
                op1=ALU.mult), waits=[prev], sig=1)
            prev = k.v(lambda e, sidx=sidx: e.tensor_scalar(
                out=sidx, in0=sidx, scalar1=4096.0, scalar2=None, op0=ALU.add),
                waits=[prev], sig=1)
    tidx = k.v(lambda e: e.tensor_copy(out=idxi[:], in_=idxf[:]),
               waits=[prev], sig=1)

    if DBG_STAGE == "idxf" and seq == 0:
        k.dma_g(ctx["dbg"][0, :, 0:64], idxf[:], waits=[prev])
        k.dma_g(ctx["dbg"][0, :, 64:224], sca[:], waits=[prev])
        k.dma_g(ctx["dbg"][0, :, 224:256], prb[:], waits=[prev])
        k.dma_g(ctx["dbg"][0, :, 256:512], gex[:], waits=[prev])
        k.dma_g(ctx["dbg"][0, :, 512:1024], msk[:], waits=[prev])
        k.dma_g(ctx["dbg"][0, :, 1024:1536], cstm[:], waits=[prev])
    if MOE_STOP == "shuf":
        stop_here()
        return
    sc_toks = []
    for ki in range(2):
        for tc in range(16):
            col = ki * 16 + tc
            sc_toks.append(k.emit("g", lambda e, col=col, tc=tc:
                e.indirect_dma_start(
                    out=bufq,
                    out_offset=bass.IndirectOffsetOnAxis(
                        ap=idxi[:, col:col + 1], axis=0),
                    in_=x_tm[:, tc * D:(tc + 1) * D], in_offset=None),
                waits=[tidx, t_xtm] if (ki == 0 and tc == 0) else (),
                sig=16, sem="gd"))
    k.barrier()
    ph_r.close()
    if MOE_STOP == "route":
        ph.close()
        return

    ph_e = ExitStack()
    sbe = lambda n, sh, dt: ph_e.enter_context(nc.sbuf_tensor(f"moe{seq}_{n}", sh, dt))
    buf_fm = sbe("bfm", [128, 8 * 256], BF16)
    ggs = sbe("ggs", [128, 4 * 256], BF16)
    hp = sbe("hp", [128, 4 * 256], BF16)
    ost = sbe("ost", [128, 2 * 1024], BF16)
    wsl = [sbe(f"wsl{i}", [128, 8 * 128], BF16) for i in range(2)]
    wpsl = sbe("wpsl", [128, 4 * 1024], BF16)
    pE = [ph_e.enter_context(nc.psum_tensor(f"mo{seq}_pE{i}", [128, 512], F32))
          for i in range(2)]

    w_free = [None, None]
    pE_free = [None, None]
    wp_free = None
    gath_pre = []
    gth_free = None
    for ex in range(NE):
        for f in range(8):
            tgth = k.dma_yt(
                buf_fm[:, f * 256:(f + 1) * 256],
                bufq[ex * CAP:(ex + 1) * CAP, f * 128:(f + 1) * 128],
                waits=(sc_toks + [gth_free]) if f == 0 else ())
        sil_toks = []
        for oc in range(4):
            wt = wsl[oc % 2]
            twl = k.dma_g(wt[:].rearrange("p (kc n) -> p kc n", n=128),
                          d["wg"][ex, oc].rearrange("kc p n -> p kc n"),
                          waits=[w_free[oc % 2]])
            bk = oc % 2
            for kc in range(8):
                tmm = k.t(lambda e, wt=wt, kc=kc, bk=bk: e.matmul(
                    pE[bk][:, 0:256], wt[:, kc * 128:(kc + 1) * 128],
                    buf_fm[:, kc * 256:(kc + 1) * 256],
                    start=(kc == 0), stop=(kc == 7), skip_group_check=True),
                    waits=[twl, tgth, pE_free[bk]] if kc == 0 else (),
                    sig=1 if kc == 7 else 0)
            w_free[oc % 2] = tmm
            ts_ = k.s(lambda e, oc=oc, bk=bk: e.activation(
                out=ggs[:, oc * 256:(oc + 1) * 256], in_=pE[bk][:, 0:256],
                func=AF.Silu), waits=[tmm], sig=1)
            sil_toks.append(ts_)
            pE_free[bk] = ts_
        hp_toks = []
        for oc in range(4):
            wt = wsl[oc % 2]
            twl = k.dma_g(wt[:].rearrange("p (kc n) -> p kc n", n=128),
                          d["wf"][ex, oc].rearrange("kc p n -> p kc n"),
                          waits=[w_free[oc % 2]])
            bk = oc % 2
            for kc in range(8):
                tmm = k.t(lambda e, wt=wt, kc=kc, bk=bk: e.matmul(
                    pE[bk][:, 0:256], wt[:, kc * 128:(kc + 1) * 128],
                    buf_fm[:, kc * 256:(kc + 1) * 256],
                    start=(kc == 0), stop=(kc == 7), skip_group_check=True),
                    waits=[twl, pE_free[bk]] if kc == 0 else (),
                    sig=1 if kc == 7 else 0)
            w_free[oc % 2] = tmm
            th_ = k.v(lambda e, oc=oc, bk=bk: e.tensor_tensor(
                out=hp[:, oc * 256:(oc + 1) * 256], in0=pE[bk][:, 0:256],
                in1=ggs[:, oc * 256:(oc + 1) * 256], op=ALU.mult),
                waits=[tmm, sil_toks[oc]], sig=1)
            hp_toks.append(th_)
            pE_free[bk] = th_
        gth_free = tmm
        twp = k.dma_g(wpsl[:].rearrange("p (kh n) -> p kh n", n=1024),
                      d["wp"][ex].rearrange("kh p n -> p kh n"),
                      waits=[wp_free])
        ost_sub = []
        tmm2 = None
        for capc in range(2):
            for dn in range(2):
                bk = (capc * 2 + dn) % 2
                for kh in range(4):
                    tmm2 = k.t(lambda e, capc=capc, dn=dn, kh=kh, bk=bk: e.matmul(
                        pE[bk][:, 0:512],
                        hp[:, kh * 256 + capc * 128: kh * 256 + capc * 128 + 128],
                        wpsl[:, kh * 1024 + dn * 512: kh * 1024 + (dn + 1) * 512],
                        start=(kh == 0), stop=(kh == 3), skip_group_check=True),
                        waits=([twp, pE_free[bk]] + hp_toks) if kh == 0 else (),
                        sig=1 if kh == 3 else 0)
                tev = k.v(lambda e, capc=capc, dn=dn, bk=bk: e.tensor_copy(
                    out=ost[:, capc * 1024 + dn * 512: capc * 1024 + (dn + 1) * 512],
                    in_=pE[bk][:, 0:512]), waits=[tmm2], sig=1)
                ost_sub.append(tev)
                pE_free[bk] = tev
        wp_free = tmm2
        st1 = k.dma_y(oall[ex * 256:ex * 256 + 128], ost[:, 0:1024], waits=ost_sub)
        st2 = k.dma_y(oall[ex * 256 + 128:(ex + 1) * 256], ost[:, 1024:2048],
                      waits=ost_sub)
        gath_pre += [st1, st2]
    k.barrier()
    ph_e.close()
    if MOE_STOP == "experts":
        ph.close()
        return

    ph_c = ExitStack()
    g1 = ph_c.enter_context(nc.sbuf_tensor(f"moc{seq}_g1", [128, 16 * D], BF16))
    g2 = ph_c.enter_context(nc.sbuf_tensor(f"moc{seq}_g2", [128, 16 * D], BF16))
    vt = ph_c.enter_context(nc.sbuf_tensor(f"moc{seq}_vt", [128, D], F32))
    for tc in range(16):
        tg1 = k.emit("g", lambda e, tc=tc: e.indirect_dma_start(
            out=g1[:, tc * D:(tc + 1) * D], out_offset=None, in_=oall,
            in_offset=bass.IndirectOffsetOnAxis(
                ap=idxi[:, 32 + tc:33 + tc], axis=0)),
            waits=gath_pre if tc == 0 else (), sig=16, sem="gd")
    for tc in range(16):
        tg2 = k.emit("g", lambda e, tc=tc: e.indirect_dma_start(
            out=g2[:, tc * D:(tc + 1) * D], out_offset=None, in_=oall,
            in_offset=bass.IndirectOffsetOnAxis(
                ap=idxi[:, 48 + tc:49 + tc], axis=0)),
            waits=(), sig=16, sem="gd")

    prevv = None
    for tc in range(16):
        ta = k.v(lambda e, tc=tc: e.tensor_scalar(
            out=vt[:], in0=g2[:, tc * D:(tc + 1) * D],
            scalar1=prb[:, 16 + tc:17 + tc], scalar2=None, op0=ALU.mult),
            waits=[tg2, prevv], sig=1)
        tb = k.v(lambda e, tc=tc: e.scalar_tensor_tensor(
            out=vt[:], in0=g1[:, tc * D:(tc + 1) * D],
            scalar=prb[:, tc:tc + 1], in1=vt[:], op0=ALU.mult, op1=ALU.add),
            waits=[ta, tg1], sig=1)
        for kc in range(8):
            tt = k.t(lambda e, tc=tc, kc=kc: e.transpose(
                pT[:, 0:128], vt[:, kc * 128:(kc + 1) * 128], idf[:]),
                waits=[tb, prevv], sig=1)
            prevv = k.v(lambda e, tc=tc, kc=kc: e.scalar_tensor_tensor(
                out=xs[kc][:, tc * 128:(tc + 1) * 128], in0=pT[:, 0:128],
                scalar=1.0, in1=xs[kc][:, tc * 128:(tc + 1) * 128],
                op0=ALU.mult, op1=ALU.add), waits=[tt], sig=1)
    ph_c.close()
    ph.close()


def build_attn(ctx, seq):
    nc, k = ctx["nc"], ctx["k"]
    xs, cst, masks = ctx["xs"], ctx["cst"], ctx["masks"]
    onec, onesq = ctx["onec"], ctx["onesq"]
    rowsc, rowbf = ctx["rowsc"], ctx["rowbf"]
    d = ctx["d"]
    ph = ExitStack()
    sb = lambda n, sh, dt: ph.enter_context(nc.sbuf_tensor(f"at{seq}_{n}", sh, dt))

    qk = sb("qk", [128, NH * T], BF16)
    vall = sb("vall", [128, 16 * NH * 65], BF16)
    k2s = sb("k2s", [64, 512], BF16)
    kmx = sb("kmx", [1, 32], F32)
    pqk = ph.enter_context(nc.psum_tensor(f"at{seq}_pqk", [128, 512], F32))
    pqk2 = ph.enter_context(nc.psum_tensor(f"at{seq}_pqk2", [128, 512], F32))
    pq2 = ph.enter_context(nc.psum_tensor(f"at{seq}_pq2", [1, 512], F32))
    pcx = [ph.enter_context(nc.psum_tensor(f"at{seq}_pcx{i}", [65, 512], F32))
           for i in range(2)]
    pbc = ph.enter_context(nc.psum_tensor(f"at{seq}_pbc", [64, 512], F32))
    pout = ph.enter_context(nc.psum_tensor(f"at{seq}_po", [128, 512], F32))


    ph2 = ExitStack()
    xbf = [ph2.enter_context(nc.sbuf_tensor(f"at{seq}_xb{c}", [128, T], BF16))
           for c in range(8)]
    slab0 = ph2.enter_context(nc.sbuf_tensor(f"at{seq}_sl0", [128, 8 * 128], BF16))
    slab = [slab0, slab0]
    txb = None
    for c in range(8):
        txb = k.v(lambda e, c=c: e.tensor_copy(out=xbf[c][:], in_=xs[c][:]), sig=1)

    slab_free = [None, None]
    bank_free = [None, None]
    for h in range(NH):
        sl = slab[h % 2]
        tsl = k.dma_g(sl[:].rearrange("p (kc n) -> p kc n", n=128), d["qkT"][h].rearrange("kc p n -> p kc n"),
                      waits=[slab_free[h % 2]])
        tmm = None
        for ts in range(4):
            bk = ts % 2
            pk = pqk if bk == 0 else pqk2
            for kc in range(8):
                tmm = k.t(lambda e, sl=sl, kc=kc, ts=ts, pk=pk: e.matmul(
                    pk[:, 0:512], sl[:, kc * 128:(kc + 1) * 128],
                    xbf[kc][:, ts * 512:(ts + 1) * 512],
                    start=(kc == 0), stop=(kc == 7), skip_group_check=True),
                    waits=[tsl, txb, bank_free[bk]] if kc == 0 else (),
                    sig=1 if kc == 7 else 0)
            tev = k.v(lambda e, h=h, ts=ts, pk=pk: e.tensor_copy(
                out=qk[:, h * T + ts * 512: h * T + (ts + 1) * 512],
                in_=pk[:, 0:512]), waits=[tmm], sig=1)
            bank_free[bk] = tev
        slab_free[0] = slab_free[1] = tmm
        if h < 8:
            continue
        trm = None
        for hf in range(2):
            hd = 2 * (h - 8) + hf
            b0 = 64 * hf
            ks = k2s[0:64, 0:512] if hf == 0 else rowbf[64:128, 0:512]
            tkm = k.v(lambda e, hd=hd: e.memset(kmx[0:1, hd:hd + 1], 0.0),
                      sig=1)
            for ts in range(4):
                tq = k.s(lambda e, h=h, ts=ts, b0=b0, ks=ks: e.activation(
                    out=ks, in_=qk[b0:b0 + 64,
                                   h * T + ts * 512: h * T + (ts + 1) * 512],
                    func=AF.Square),
                    waits=([bank_free[0], bank_free[1], trm]
                           if ts == 0 else [trm]), sig=1)
                tn = k.t(lambda e, b0=b0, ks=ks: e.matmul(
                    pq2[:], onec[b0:b0 + 64, :], ks, start=True, stop=True,
                    skip_group_check=True), waits=[tq], sig=1)
                trm = k.v(lambda e: e.reduce_max(out=kmx[0:1, 16:17],
                          in_=pq2[0:1, :], axis=AXX), waits=[tn, tkm], sig=1)
                tkm = k.v(lambda e, hd=hd: e.tensor_tensor(
                    out=kmx[0:1, hd:hd + 1], in0=kmx[0:1, hd:hd + 1],
                    in1=kmx[0:1, 16:17], op=ALU.max), waits=[trm], sig=1)
            ctx.setdefault("at_kmx", {})[hd] = tkm

    tvz = None
    vdone = []
    for vb in range(8):
        sl = slab[vb % 2]
        tvl = k.dma_g(sl[:].rearrange("p (kc n) -> p kc n", n=128),
                      d["vT"][:, :, vb * 128:(vb + 1) * 128].rearrange(
                          "kc p n -> p kc n"),
                      waits=[slab_free[vb % 2]])
        for tc in range(16):
            bk = tc % 2
            pk = pqk if bk == 0 else pqk2
            for kc in range(8):
                tmm = k.t(lambda e, sl=sl, kc=kc, tc=tc, pk=pk: e.matmul(
                    pk[:, 0:128], xbf[kc][:, tc * 128:(tc + 1) * 128],
                    sl[:, kc * 128:(kc + 1) * 128],
                    start=(kc == 0), stop=(kc == 7), skip_group_check=True),
                    waits=[tvl, bank_free[bk]] if kc == 0 else (),
                    sig=1 if kc == 7 else 0)
            t1 = k.v(lambda e, tc=tc, vb=vb, pk=pk: e.tensor_copy(
                out=vall[:, tc * NH * 65 + (2 * vb) * 65:
                         tc * NH * 65 + (2 * vb) * 65 + 64],
                in_=pk[:, 0:64]), waits=[tmm], sig=1)
            t2 = k.v(lambda e, tc=tc, vb=vb, pk=pk: e.tensor_copy(
                out=vall[:, tc * NH * 65 + (2 * vb + 1) * 65:
                         tc * NH * 65 + (2 * vb + 1) * 65 + 64],
                in_=pk[:, 64:128]), waits=[t1], sig=1)
            vdone.append(t2)
            bank_free[bk] = t2
        slab_free[0] = slab_free[1] = tmm
    tone = k.v(lambda e: e.memset(
        vall[:].rearrange("p (th n) -> p th n", n=65)[:, :, 64:65], 1.0),
        waits=[vdone[-1]], sig=1)
    k.barrier()
    ph2.close()

    ptil = sb("ptil", [128, 2 * 512], BF16)
    ctxa = sb("ctxa", [64, NH * 512], BF16)
    wosl = [sb(f"wo{i}", [64, 16 * 128], BF16) for i in range(2)]

    two_free = [None, None]
    pcx_free = [None, None]
    psum_free = [None, None]
    ptil_free = [None, None]
    brow_free = None
    last_evac = None
    prev_add = None
    for qb in range(4):
        q0 = qb * 512
        nkc = (qb + 1) * 4
        for h in range(NH):
            p_, hf = h // 2, h % 2
            b0 = 64 * hf
            ks = k2s[0:64, 0:512] if hf == 0 else rowbf[64:128, 0:512]
            tq2 = k.s(lambda e, p_=p_, b0=b0, ks=ks, q0=q0: e.activation(
                out=ks, in_=qk[b0:b0 + 64, p_ * T + q0: p_ * T + q0 + 512],
                func=AF.Square), waits=[last_evac], sig=1)
            tn = k.t(lambda e, b0=b0, ks=ks: e.matmul(
                pq2[:], onec[b0:b0 + 64, :], ks,
                start=True, stop=True, skip_group_check=True),
                waits=[tq2], sig=1)
            tb1 = k.s(lambda e: e.activation(out=rowsc[0:1, 0:512], in_=pq2[0:1, :],
                      func=AF.Ln, bias=cst[0:1, C_TINY38:C_TINY38 + 1]),
                      waits=[tn], sig=1)
            tb2 = k.s(lambda e: e.activation(out=rowsc[0:1, 0:512],
                      in_=rowsc[0:1, 0:512], func=AF.Exp, scale=0.5),
                      waits=[tb1], sig=1)
            tb3 = k.v(lambda e, h=h: e.tensor_scalar(
                out=rowbf[0:1, 0:512], in0=rowsc[0:1, 0:512],
                scalar1=kmx[0:1, h:h + 1], scalar2=-0.125, op0=ALU.mult,
                op1=ALU.mult), waits=[tb2, brow_free, ctx["at_kmx"][h]], sig=1)
            exp_toks = [None] * nkc
            pc = pcx[(qb * NH + h) % 2]
            pvw0 = [pcx_free[(qb * NH + h) % 2], tone]
            tpv_done = None
            for tkc in range(nkc):
                bk = tkc % 2
                pk = pqk if bk == 0 else pqk2
                k.t(lambda e, p_=p_, b0=b0, tkc=tkc, q0=q0, pk=pk: e.matmul(
                    pk[:, 0:512],
                    qk[b0:b0 + 64,
                       (8 + p_) * T + tkc * 128: (8 + p_) * T + (tkc + 1) * 128],
                    qk[b0:b0 + 64, p_ * T + q0: p_ * T + q0 + 512],
                    start=True, stop=False, skip_group_check=True),
                    waits=[psum_free[bk], tb3])
                brow_free = k.t(lambda e, pk=pk: e.matmul(
                    pk[:, 0:512], onesq[0:1, :], rowbf[0:1, 0:512],
                    start=False, stop=True, skip_group_check=True), sig=1)
                tex = k.s(lambda e, tkc=tkc, pk=pk, bk=bk: e.activation(
                    out=ptil[:, bk * 512:(bk + 1) * 512], in_=pk[:, 0:512],
                    func=AF.Exp, scale=0.125),
                    waits=[brow_free, ptil_free[bk]], sig=1)
                psum_free[bk] = tex
                rel = tkc * 128 - q0
                if 0 <= rel < 512:
                    tex = k.v(lambda e, bk=bk, rel=rel: e.tensor_tensor(
                        out=ptil[:, bk * 512:(bk + 1) * 512],
                        in0=ptil[:, bk * 512:(bk + 1) * 512],
                        in1=masks[:, (rel // 128) * 512:(rel // 128 + 1) * 512],
                        op=ALU.mult), waits=[tex], sig=1)
                exp_toks[tkc] = tex
                tpv = k.t(lambda e, h=h, tkc=tkc, pc=pc, nkc=nkc: e.matmul(
                    pc[:, 0:512],
                    vall[:, tkc * NH * 65 + h * 65: tkc * NH * 65 + (h + 1) * 65],
                    ptil[:, (tkc % 2) * 512:((tkc % 2) + 1) * 512],
                    start=(tkc == 0), stop=(tkc == nkc - 1), skip_group_check=True),
                    waits=(pvw0 + [tex]) if tkc == 0 else [tex], sig=1)
                ptil_free[bk] = tpv
                if tkc == nkc - 1:
                    tpv_done = tpv
            ts1 = k.s(lambda e, pc=pc: e.activation(
                out=rowsc[64:65, 0:512], in_=pc[64:65, 0:512], func=AF.Ln,
                bias=cst[0:1, C_TINY38:C_TINY38 + 1]), waits=[tpv_done], sig=1)
            ts2 = k.s(lambda e: e.activation(out=rowbf[64:65, 512:1024],
                      in_=rowsc[64:65, 0:512], func=AF.Exp, scale=-1.0),
                      waits=[ts1], sig=1)
            tbc = k.t(lambda e: e.matmul(pbc[:, 0:512], onesq[64:65, 0:64],
                      rowbf[64:65, 512:1024], start=True, stop=True,
                      skip_group_check=True), waits=[ts2, last_evac], sig=1)
            tev = k.v(lambda e, h=h, pc=pc: e.tensor_copy(
                out=ctxa[:, h * 512:(h + 1) * 512], in_=pc[0:64, 0:512]),
                waits=[tpv_done], sig=1)
            last_evac = k.v(lambda e, h=h: e.tensor_tensor(
                out=ctxa[:, h * 512:(h + 1) * 512],
                in0=ctxa[:, h * 512:(h + 1) * 512],
                in1=pbc[:, 0:512], op=ALU.mult), waits=[tev, tbc], sig=1)
            pcx_free[(qb * NH + h) % 2] = last_evac

        for oc in range(8):
            wo = wosl[oc % 2]
            two = k.dma_g(wo[:].rearrange("p (kh n) -> p kh n", n=128),
                          d["woT"][oc].rearrange("kh p n -> p kh n"),
                          waits=[two_free[oc % 2]])
            for kh in range(16):
                tmm = k.t(lambda e, wo=wo, kh=kh: e.matmul(
                    pout[:, 0:512], wo[:, kh * 128:(kh + 1) * 128],
                    ctxa[:, kh * 512:(kh + 1) * 512],
                    start=(kh == 0), stop=(kh == 15), skip_group_check=True),
                    waits=[two, last_evac, prev_add] if kh == 0 else (),
                    sig=1 if kh == 15 else 0)
            two_free[oc % 2] = tmm
            prev_add = k.v(lambda e, oc=oc, q0=q0: e.scalar_tensor_tensor(
                out=xs[oc][:, q0:q0 + 512], in0=pout[:, 0:512], scalar=1.0,
                in1=xs[oc][:, q0:q0 + 512], op0=ALU.mult, op1=ALU.add),
                waits=[tmm], sig=1)
        last_evac = prev_add
    ph.close()


def build_ffn(ctx, seq):
    nc, k = ctx["nc"], ctx["k"]
    xs, cst, b1t, b2q = ctx["xs"], ctx["cst"], ctx["b1t"], ctx["b2q"]
    d, wqd = ctx["d"], ctx["wqd"]
    ph = ExitStack()
    sb = lambda n, sh, dt: ph.enter_context(nc.sbuf_tensor(f"ff{seq}_{n}", sh, dt))
    xbf = [sb(f"xb{c}", [128, T], BF16) for c in range(8)]
    hbuf = sb("h", [128, 16 * 256], F32)
    hq = sb("hq", [128, 32 * 256], BF16)
    v0 = sb("v0", [128, 8 * 256], F32)
    qsl = sb("qsl", [128, 8 * 256], F32)
    slab = [sb(f"sl{i}", [128, 8 * 128], BF16) for i in range(2)]
    wsl = sb("ws", [128, 32 * 128], BF16)
    qs = make_qscratch(nc, ph, f"ff{seq}", 2048)
    pm = [ph.enter_context(nc.psum_tensor(f"ff{seq}_p{i}", [128, 256], F32))
          for i in range(4)]

    txb = None
    for c in range(8):
        txb = k.v(lambda e, c=c: e.tensor_copy(out=xbf[c][:], in_=xs[c][:]), sig=1)

    slab_free = [None, None]
    wsl_free = None
    bank_free = [None] * 4
    prev_q = None
    hq_free = None
    src_free = []
    for sl in range(8):
        tok0 = sl * 256
        hq_toks = []
        for hoc in range(2):
            ev_toks = []
            for oc16 in range(16):
                oc = hoc * 16 + oc16
                sw = slab[oc % 2]
                tsl = k.dma_g(sw[:].rearrange("p (kc n) -> p kc n", n=128),
                              d["w1T"][oc].rearrange("kc p n -> p kc n"),
                              waits=[slab_free[oc % 2]])
                bk = oc % 4
                for kc in range(8):
                    tmm = k.t(lambda e, sw=sw, kc=kc, bk=bk, tok0=tok0: e.matmul(
                        pm[bk][:], sw[:, kc * 128:(kc + 1) * 128],
                        xbf[kc][:, tok0:tok0 + 256],
                        start=(kc == 0), stop=(kc == 7), skip_group_check=True),
                        waits=[tsl, txb, bank_free[bk]] if kc == 0 else (),
                        sig=1 if kc == 7 else 0)
                slab_free[oc % 2] = tmm
                tev = k.s(lambda e, oc16=oc16, oc=oc, bk=bk: e.activation(
                    out=hbuf[:, oc16 * 256:(oc16 + 1) * 256], in_=pm[bk][:],
                    func=AF.Relu, bias=b1t[:, oc:oc + 1]),
                    waits=[tmm] + src_free, sig=1)
                bank_free[bk] = tev
                ev_toks.append(tev)
            pairs = [(hbuf[:, i * 2048:(i + 1) * 2048],
                      hq[:, hoc * 4096 + i * 2048: hoc * 4096 + (i + 1) * 2048])
                     for i in range(2)]
            tq, src_free = emit_quant(ctx, qs, pairs, 4, 3, signed=False,
                                      wait0=ev_toks + [hq_free])
            hq_toks.append(tq)
        v0_toks = []
        for oc8 in range(8):
            twl = k.dma_g(wsl[:].rearrange("p (kc n) -> p kc n", n=128),
                          wqd[oc8].rearrange("kc p n -> p kc n"),
                          waits=[wsl_free])
            bk = oc8 % 4
            for kc in range(32):
                tmm = k.t(lambda e, kc=kc, bk=bk: e.matmul(
                    pm[bk][:], wsl[:, kc * 128:(kc + 1) * 128],
                    hq[:, kc * 256:(kc + 1) * 256],
                    start=(kc == 0), stop=(kc == 31), skip_group_check=True),
                    waits=([twl, bank_free[bk]] + hq_toks) if kc == 0 else (),
                    sig=1 if kc == 31 else 0)
            wsl_free = tmm
            tev = k.v(lambda e, oc8=oc8, bk=bk: e.tensor_scalar(
                out=v0[:, oc8 * 256:(oc8 + 1) * 256], in0=pm[bk][:],
                scalar1=b2q[:, oc8:oc8 + 1], scalar2=None, op0=ALU.add),
                waits=[tmm], sig=1)
            bank_free[bk] = tev
            v0_toks.append(tev)
        hq_free = tmm
        tq5, sf5 = emit_quant(ctx, qs, [(v0[:], qsl[:])], 5, 10, signed=True,
                              wait0=v0_toks + src_free + [prev_q])
        src_free = src_free + sf5
        for oc8 in range(8):
            prev_q = k.v(lambda e, oc8=oc8, tok0=tok0: e.scalar_tensor_tensor(
                out=xs[oc8][:, tok0:tok0 + 256], in0=qsl[:, oc8 * 256:(oc8 + 1) * 256],
                scalar=1.0, in1=xs[oc8][:, tok0:tok0 + 256],
                op0=ALU.mult, op1=ALU.add), waits=[tq5], sig=1)
        src_free = src_free + [prev_q]
    ph.close()


def build_w2q(ctx):
    nc, k = ctx["nc"], ctx["k"]
    d, wqd, cst, b2q = ctx["d"], ctx["wqd"], ctx["cst"], ctx["b2q"]
    ph = ExitStack()
    wbuf = [ph.enter_context(nc.sbuf_tensor(f"wq_in{i}", [128, 1024], F32))
            for i in range(2)]
    wout = [ph.enter_context(nc.sbuf_tensor(f"wq_out{i}", [128, 1024], BF16))
            for i in range(2)]
    b2s = ph.enter_context(nc.sbuf_tensor("wq_b2", [128, 8], F32))
    qs = make_qscratch(nc, ph, "w2", 1024)

    tst = [None, None]
    for kc in range(32):
        i = kc % 2
        tl = k.dma_y(wbuf[i][:], d["w2T"][kc], waits=[tst[i]])
        tq, _sf = emit_quant(ctx, qs, [(wbuf[i][:, 0:1024], wout[i][:, 0:1024])],
                             4, 3, signed=True, wait0=[tl, tst[i]])
        tst[i] = k.dma_y(wqd[:, kc].rearrange("oc p n -> p oc n"),
                         wout[i][:].rearrange("p (oc n) -> p oc n", n=128), waits=[tq])
    tb = k.dma_y(b2s[:], d["b2"])
    emit_quant(ctx, qs, [(b2s[:], b2q[:])], 5, 10, signed=True, wait0=[tb])
    ph.close()


def host_prep(inputs):
    iw = inputs
    w = {}

    def tiles_T(m, OC, KC):
        return np.ascontiguousarray(
            np.asarray(m, np.float32).reshape(OC, 128, KC, 128).transpose(0, 2, 3, 1))

    w["iwT"] = tiles_T(iw["L0_attn_input_w"], 24, 8)
    w["gwT"] = tiles_T(iw["L0_attn_gates_w"], 24, 12)
    w["owT"] = tiles_T(iw["L0_attn_output_w"], 8, 12)
    cw = np.asarray(iw["L0_attn_conv_w"], np.float32)
    cd = np.zeros((12, 4, 128, 128), np.float32)
    for cc in range(12):
        for kk in range(4):
            np.fill_diagonal(cd[cc, kk], cw[cc * 128:(cc + 1) * 128, 0, kk])
    w["cdiag"] = cd
    w["cb"] = np.ascontiguousarray(
        np.asarray(iw["L0_attn_conv_b"], np.float32).reshape(12, 128).T)
    w["fb"] = np.ascontiguousarray(
        np.asarray(iw["L0_attn_forget_base"], np.float32).reshape(12, 128).T)
    w["wrt"] = np.ascontiguousarray(
        np.asarray(iw["L0_ffn_w_router"], np.float32).reshape(8, 128, 16))
    wg = np.asarray(iw["L0_ffn_w_gate"], np.float32)
    w["wg"] = np.ascontiguousarray(
        wg.reshape(16, 8, 128, 4, 128).transpose(0, 3, 1, 2, 4))
    wf = np.asarray(iw["L0_ffn_w_fc"], np.float32)
    w["wf"] = np.ascontiguousarray(
        wf.reshape(16, 8, 128, 4, 128).transpose(0, 3, 1, 2, 4))
    w["wp"] = np.ascontiguousarray(
        np.asarray(iw["L0_ffn_w_proj"], np.float32).reshape(16, 4, 128, 1024))
    wqkvT = np.ascontiguousarray(np.asarray(iw["L1_attn_w_qkv"], np.float32).T)
    qkT = np.zeros((NH, 8, 128, 128), np.float32)
    for p in range(8):
        for hf in range(2):
            h = 2 * p + hf
            qkT[p, :, :, 64 * hf:64 * (hf + 1)] = \
                wqkvT[:, 64 * h:64 * (h + 1)].reshape(8, 128, 64)
            qkT[8 + p, :, :, 64 * hf:64 * (hf + 1)] = \
                wqkvT[:, 1024 + 64 * h:1024 + 64 * (h + 1)].reshape(8, 128, 64)
    w["qkT"] = qkT
    w["vT"] = np.ascontiguousarray(wqkvT[:, 2048:3072].reshape(8, 128, 1024))
    wo = np.asarray(iw["L1_attn_w_out"], np.float32)
    w["woT"] = np.ascontiguousarray(
        wo.reshape(8, 128, 16, 64).transpose(0, 2, 3, 1))
    w["w1T"] = tiles_T(iw["L1_ffn_w1"], 32, 8)
    w["w2T"] = np.ascontiguousarray(
        np.asarray(iw["L1_ffn_w2"], np.float32).T.reshape(32, 128, 1024))
    w["b1"] = np.ascontiguousarray(
        np.asarray(iw["L1_ffn_b1"], np.float32).reshape(32, 128).T)
    w["b2"] = np.ascontiguousarray(
        np.asarray(iw["L1_ffn_b2"], np.float32).reshape(8, 128).T)
    w["consts"] = np.tile(np.array(CONST_VALS, np.float32), (128, 1))
    w["ones_col"] = np.ones((128, 1), np.float32)
    w["onesq"] = np.ones((128, 128), np.float32)
    w["ident"] = np.eye(128, dtype=np.float32)
    mk = np.zeros((4, 128, 512), np.float32)
    for r in range(4):
        for i in range(128):
            mk[r, i, r * 128 + i:] = 1.0
    w["masks"] = np.ascontiguousarray(mk.transpose(1, 0, 2).reshape(128, 4 * 512))
    w["iota16"] = np.tile(np.arange(16, dtype=np.float32), (128, 1))

    x = np.asarray(iw["L0_attn_x"], np.float32)
    in_maps = []
    ncore = x.shape[0] // NSEQ
    for core in range(ncore):
        m = dict(w)
        xc = x[NSEQ * core:NSEQ * (core + 1)]
        m["x"] = np.ascontiguousarray(
            xc.transpose(0, 2, 1).reshape(NSEQ, 8, 128, T))
        in_maps.append(m)
    return in_maps


_NC_CACHE = {}


def kernel(nblocks=4, cores=8, **inputs):
    in_maps = host_prep(inputs)[:cores]
    key = (nblocks,)
    if key not in _NC_CACHE:
        _NC_CACHE[key] = build_nc(nblocks)
    nc = _NC_CACHE[key]
    res = run_bass_kernel_spmd(nc, in_maps, core_ids=list(range(len(in_maps))))
    outs = []
    for r in res.results:
        yc = np.asarray(r["y"]).reshape(NSEQ, 1024, T).transpose(0, 2, 1)
        outs.append(yc)
    return np.ascontiguousarray(np.concatenate(outs, axis=0))
